# version 18
# baseline (speedup 1.0000x reference)
import os
import numpy as np
import ml_dtypes
import concourse.bass as bass
import concourse.mybir as mybir
from concourse.bass_utils import run_bass_kernel_spmd

F32 = mybir.dt.float32
BF16 = mybir.dt.bfloat16
U8 = mybir.dt.uint8
AF = mybir.ActivationFunctionType
ALU = mybir.AluOpType

NCORES = 8
T = 2048
D = 1024
FF = 4096
H = 16
HD = 64
PT = D // 128
FT = FF // 128
KT = 8
CH = 512
NCH = T // CH
MAGIC = 12582912.0
TWOPI = float(2 * np.pi)
INV2PI = float(1.0 / (2 * np.pi))
INVD = 1.0 / D
EPS = 1e-5


def _chunks512(a, b):
    out = []
    while a < b:
        e = min(b, (a // 512 + 1) * 512)
        out.append((a, e))
        a = e
    return out


def build_nc(seqs=4):
    nc = bass.Bass("TRN2", target_bir_lowering=False,
                   detect_race_conditions=os.environ.get("KMK_RACES", "") == "1")

    xt_d = nc.dram_tensor("xt", [seqs, D, T], F32, kind="ExternalInput").ap()
    qw_d = nc.dram_tensor("qwp", [D, D], F32, kind="ExternalInput").ap()
    kw_d = nc.dram_tensor("kwp", [D, D], F32, kind="ExternalInput").ap()
    vw_d = nc.dram_tensor("vwp", [D, D], F32, kind="ExternalInput").ap()
    ow_d = nc.dram_tensor("owp", [D, D], F32, kind="ExternalInput").ap()
    w1_d = nc.dram_tensor("w1", [D, FF], F32, kind="ExternalInput").ap()
    w2_d = nc.dram_tensor("w2", [FF, D], F32, kind="ExternalInput").ap()
    b1_d = nc.dram_tensor("b1g", [128, FT], F32, kind="ExternalInput").ap()
    th_d = nc.dram_tensor("thg", [128, FT], F32, kind="ExternalInput").ap()
    wr_d = nc.dram_tensor("wrg", [128, FT], F32, kind="ExternalInput").ap()
    wi_d = nc.dram_tensor("wig", [128, FT], F32, kind="ExternalInput").ap()
    b2_d = nc.dram_tensor("b2g", [128, PT], F32, kind="ExternalInput").ap()
    tri_d = nc.dram_tensor("tri", [128, 128], BF16, kind="ExternalInput").ap()
    idm_d = nc.dram_tensor("idm", [128, 128], BF16, kind="ExternalInput").ap()
    mcm_d = nc.dram_tensor("mcm", [128, 128], BF16, kind="ExternalInput").ap()
    b2r_d = nc.dram_tensor("b2row", [1, D], F32, kind="ExternalInput").ap()
    yt_d = nc.dram_tensor("yt", [seqs, D, T], F32, kind="ExternalOutput").ap()
    DEBUG = os.environ.get("KMK_DEBUG", "") == "1"
    if DEBUG:
        dbg_ln1 = nc.dram_tensor("dbg_ln1", [D, T], BF16, kind="ExternalOutput").ap()
        dbg_f1 = nc.dram_tensor("dbg_f1", [D, T], F32, kind="ExternalOutput").ap()
        dbg_att = nc.dram_tensor("dbg_att", [D, T], F32, kind="ExternalOutput").ap()
        dbg_w1 = nc.dram_tensor("dbg_w1", [D, 512], BF16, kind="ExternalOutput").ap()
        dbg_w2 = nc.dram_tensor("dbg_w2", [512, 128], BF16, kind="ExternalOutput").ap()
        dbg_act = nc.dram_tensor("dbg_act", [128, 16, 512], BF16, kind="ExternalOutput").ap()
        dbg_vg = nc.dram_tensor("dbg_vg", [128, 8, 32], F32, kind="ExternalOutput").ap()
        dbg_hid = nc.dram_tensor("dbg_hid", [128, 512], F32, kind="ExternalOutput").ap()

    w1b_d = nc.dram_tensor("w1b", [16, 128, KT, 256], BF16, kind="Internal").ap()
    w2b_d = nc.dram_tensor("w2b", [PT, 2, 128, 16, 128], BF16, kind="Internal").ap()
    qwb_d = nc.dram_tensor("qwb", [8, 128, KT, 128], BF16, kind="Internal").ap()
    kwb_d = nc.dram_tensor("kwb", [8, 128, KT, 128], BF16, kind="Internal").ap()
    vwb_d = nc.dram_tensor("vwb", [128, KT, D], BF16, kind="Internal").ap()
    owb_d = nc.dram_tensor("owb", [2, 128, 4, D], BF16, kind="Internal").ap()

    from contextlib import ExitStack
    st = ExitStack()
    nc._exit_stack = st

    y = st.enter_context(nc.sbuf_tensor("y", [128, PT, T], F32))
    lnout = st.enter_context(nc.sbuf_tensor("lnout", [128, PT, T], BF16))
    POOLB = 95 * 1024
    pool_t = st.enter_context(nc.sbuf_tensor("pool", [128, POOLB], U8))
    pool = pool_t[:]
    _off = [0]

    def carve(nbytes, dtype, shape=None, off=None):
        o = (_off[0] + 63) // 64 * 64 if off is None else off
        end = o + nbytes
        if off is None:
            _off[0] = end
        assert end <= POOLB, f"pool overflow {end}"
        v = pool[:, o:o + nbytes].bitcast(dtype)
        if shape is not None:
            names = [chr(97 + i) for i in range(len(shape))]
            v = v.rearrange("p (" + " ".join(names) + ") -> p " + " ".join(names),
                            **{n: s for n, s in zip(names, shape)})
        return v, o

    rows, rows_off = carve(2048 * 4, F32, [4, 512])
    b2rb, _ = carve(1024 * 2, BF16)
    rec, _ = carve(2048 * 4, F32, [2, 1024], off=rows_off)
    vg, _ = carve(8 * 32 * 4, F32, [8, 32])
    b2v, _ = carve(2 * 8 * 4, F32, [2, 8])
    sc1, _ = carve(8 * 4, F32)
    onesr, _ = carve(128 * 4, F32)
    onesb, _ = carve(512 * 2, BF16)
    trib, _ = carve(128 * 2, BF16)
    idmb, _ = carve(128 * 2, BF16)
    mcmb, _ = carve(128 * 2, BF16)
    emask, _ = carve(2 * 128 * 2, BF16, [2, 128])
    ysqb, _ = carve(2 * 512 * 2, BF16, [2, 512])
    reccol, _ = carve(4 * 4 * 4, F32, [4, 4])
    rcb, _ = carve(4 * 4 * 64 * 2, BF16, [4, 4, 64])
    oh8f, _ = carve(8 * 8 * 2, BF16, [8, 8])
    sh0 = (_off[0] + 63) // 64 * 64
    t1b, _ = carve(512 * 4, F32, off=sh0 + 4096)
    chu, _ = carve(2 * 512 * 4, F32, [2, 512], off=sh0)
    cht, _ = carve(2 * 512 * 4, F32, [2, 512], off=sh0 + 4096)
    chrb, _ = carve(2 * 512 * 4, F32, [2, 512], off=sh0 + 8192)
    pvsb, _ = carve(4 * 512 * 2, BF16, [4, 512], off=sh0)
    _off[0] = sh0 + 12288

    ph0 = (_off[0] + 63) // 64 * 64
    _off[0] = ph0
    act, _ = carve(2 * 16 * CH * 2, BF16, [2, 16, CH])
    w1e, _ = carve(4 * KT * 256 * 2, BF16, [4, KT, 256])
    w2e, _ = carve(4 * 16 * 128 * 2, BF16, [4, 16, 128])
    ffn_end = _off[0]
    Ebuf, _ = carve(4 * 512 * 2, BF16, [4, 512], off=sh0 + 8192)
    _off[0] = ph0
    Vau, _ = carve(H * H * 65 * 2, BF16, [H, H, 65])
    qkT, _ = carve(2 * T * 2, BF16, [2, T])
    attnTh, at_off = carve(4 * T * 2, BF16, [4, T])
    strm, _ = carve(2 * 2 * KT * 128 * 2, BF16, [2, 2, KT, 128])
    attn_end = _off[0]
    vwbuf = pool[:, at_off:at_off + 4 * T * 2].bitcast(BF16).rearrange(
        "p (k m) -> p k m", k=KT, m=D)
    stgf, _ = carve(2 * 4096 * 4, F32, off=ph0)
    stgb, _ = carve(2 * 4096 * 2, BF16, off=ph0 + 32768)
    _off[0] = max(ffn_end, attn_end)

    pall = st.enter_context(nc.psum_tensor("pall", [128, 4096], F32))

    def BANK(b, r0=0, r1=128, c0=0, c1=512):
        return pall[r0:r1, 512 * b + c0:512 * b + c1]

    sems = {n: st.enter_context(nc.semaphore(n)) for n in
            ("dP", "dPS", "dW1", "dW2", "dQ", "dX", "dA", "st", "sv", "ss")}
    cnt = {n: 0 for n in sems}
    prog = {k: [] for k in ("sync", "gpsimd", "vector", "scalar", "tensor")}

    def E(eng, meth, *args, waits=(), inc=None, **kw):
        ws = tuple((s, v) for (s, v) in waits if v > 0)
        prog[eng].append((meth, args, kw, ws, inc))

    def bump(sem, n=1):
        cnt[sem] += n
        return (sem, cnt[sem])

    def at(sem):
        return (sem, cnt[sem])

    DMA_SEMS = ("dP", "dW1", "dW2", "dQ", "dX", "dA")

    def dma(stream, eng, out, in_, waits=()):
        E(eng, "dma_start", waits=waits, inc=bump(stream, 16), out=out, in_=in_)
        return at(stream)

    def mm(out, lhsT, rhs, start, stop, waits=(), tp=None, inc=True):
        kw = {"start": start, "stop": stop}
        if tp is not None:
            kw["tile_position"] = tp
        E("tensor", "matmul", out, lhsT, rhs, waits=waits,
          inc=(bump("st") if inc else None), **kw)
        return at("st")

    Z = ("sv", 0)

    pieces = []
    w1r = w1_d.rearrange("(kt p) f -> kt p f", p=128)
    for k in range(KT):
        dsts = [(w1b_d[pc, :, k, :], pc * 256, (pc + 1) * 256)
                for pc in range(16)]
        pieces.append((w1r[k], dsts, FF, 1.0, "w1"))
    w2r = w2_d.rearrange("(ft p) d -> ft p d", p=128)
    for f in range(FT):
        dsts = [(w2b_d[dt, f // 16, :, f % 16, :], dt * 128, (dt + 1) * 128)
                for dt in range(PT)]
        pieces.append((w2r[f], dsts, D, 0.5, "w2"))
    for nm, src_, dst in (("qk", qw_d, qwb_d), ("qk", kw_d, kwb_d)):
        sr = src_.rearrange("(kt p) d -> kt p d", p=128)
        for k in range(KT):
            dsts = [(dst[pr, :, k, :], pr * 128, (pr + 1) * 128)
                    for pr in range(8)]
            pieces.append((sr[k], dsts, D, 1.0, nm))
    vr = vw_d.rearrange("(kt p) d -> kt p d", p=128)
    for k in range(KT):
        pieces.append((vr[k], [(vwb_d[:, k, :], 0, D)], D, 1.0, "vw"))
    orr = ow_d.rearrange("(kt p) d -> kt p d", p=128)
    for k in range(KT):
        pieces.append((orr[k], [(owb_d[k // 4, :, k % 4, :], 0, D)], D, 1.0,
                       "ow"))

    stg_conv = [Z, Z]
    ready_tok = {}
    for i, (src_, dsts, w, scale, grp) in enumerate(pieces):
        b = i % 2
        o = b * 4096
        ld = dma("dP", "sync", stgf[:, o:o + w], src_, waits=(stg_conv[b],))
        all_stores = at("dPS")
        if scale == 1.0:
            E("vector", "tensor_copy", stgb[:, o:o + w], stgf[:, o:o + w],
              waits=(ld, all_stores), inc=bump("sv"))
        else:
            E("vector", "tensor_single_scalar", stgb[:, o:o + w],
              stgf[:, o:o + w], scale, ALU.mult,
              waits=(ld, all_stores), inc=bump("sv"))
        stg_conv[b] = at("sv")
        for dst_ap, c0, c1 in dsts:
            dma("dPS", "sync", dst_ap, stgb[:, o + c0:o + c1],
                waits=(stg_conv[b],))
        ready_tok[grp] = at("dPS")
    W1_READY = ready_tok["w1"]
    W2_READY = ready_tok["w2"]
    QK_READY = ready_tok["qk"]
    VW_READY = ready_tok["vw"]
    OW_READY = ready_tok["ow"]

    xload = dma("dX", "gpsimd", y[:],
                xt_d[0].rearrange("(pt p) t -> p pt t", p=128))
    for dst, src in ((vg[:, 0, :], b1_d), (vg[:, 1, :], th_d),
                     (vg[:, 2, :], wr_d), (vg[:, 3, :], wi_d),
                     (b2v[:, 0, :], b2_d), (trib, tri_d),
                     (idmb, idm_d), (mcmb, mcm_d),
                     (rec[0:1, 0, :], b2r_d)):
        dma("dA", "gpsimd", dst, src)
    AUX = at("dA")

    E("vector", "memset", sc1[:, 0:1], -3.0, waits=(AUX,))
    E("vector", "memset", sc1[:, 1:2], 0.0)
    E("vector", "memset", sc1[:, 2:3], EPS)
    E("vector", "memset", sc1[:, 3:4], MAGIC)
    E("vector", "memset", sc1[:, 4:5], -MAGIC)
    E("vector", "memset", onesr[:, 0:1], 1.0)
    E("vector", "memset", onesr[0:1, :], 1.0)
    E("vector", "memset", onesr[64:65, 0:64], 1.0)
    E("vector", "memset", onesb[:, 0:1], 1.0)
    E("vector", "memset", onesb[0:1, :], 1.0)
    E("vector", "memset", onesb[:, 64:128], 1.0)
    E("vector", "memset", oh8f[:], 0.0)
    for _j in range(8):
        E("vector", "memset", oh8f[64:65, _j, _j:_j + 1], 1.0)
    E("vector", "tensor_single_scalar", b2rb[0:1, :], rec[0:1, 0, :], 0.5,
      ALU.mult)
    expb = sc1[:, 0:1]
    zb = sc1[:, 1:2]
    epsb = sc1[0:1, 2:3]
    magb = sc1[:, 3:4]
    nmagb = sc1[:, 4:5]
    ones_colb = onesb[:, 0:1]
    ones_rowb = onesb[0:1, :]
    ones_col = onesr[:, 0:1]
    ones_row = onesr[0:1, 0:128]
    ones64 = onesr[64:65, 0:64]
    b1g, thg, wrg, wig = (vg[:, i, :] for i in range(4))
    beta, cg, tm1, tm2 = (vg[:, i, :] for i in range(4, 8))
    def vstep(meth, *args, **kw):
        E("vector", meth, *args, waits=(at("sv"),), inc=bump("sv"), **kw)

    E("vector", "scalar_tensor_tensor", beta, b1g, 0.1, thg, ALU.mult, ALU.add)
    vstep("tensor_scalar", tm1, thg, INV2PI, MAGIC, ALU.mult, ALU.add)
    vstep("tensor_single_scalar", tm2, tm1, MAGIC, ALU.subtract)
    vstep("scalar_tensor_tensor", tm1, tm2, -TWOPI, thg, ALU.mult, ALU.add)
    VD = at("sv")
    E("scalar", "activation", tm2, tm1, AF.Sin, waits=(VD,), bias=zb,
      scale=1.0 - 1e-6)
    E("scalar", "activation", cg, tm1, AF.Sin, bias=zb, scale=0.5 * (1 - 1e-6),
      inc=bump("ss"))
    SD = at("ss")
    E("vector", "tensor_mul", tm1, cg, cg, waits=(SD,), inc=bump("sv"))
    vstep("tensor_scalar", cg, tm1, -2.0, 1.0, ALU.mult, ALU.add)
    vstep("tensor_mul", tm1, cg, wrg)
    vstep("tensor_mul", tm2, tm2, wig)
    vstep("tensor_add", tm1, tm1, tm2)
    vstep("scalar_tensor_tensor", cg, tm1, 0.1, wig, ALU.mult, ALU.mult)
    vstep("engine_nop")
    PAR = at("sv")

    bank_free = [Z] * 8
    dbg_first = [True]
    dbg_hid_first = [True]

    def emit_ln(y_toks, lnout_free):
        ysq_tok = [("st", 0)] * 2
        stats_tok = [None] * NCH
        apply_toks = []
        rows_free = ("st", 0)

        def stats(c):
            cc = slice(c * CH, (c + 1) * CH)
            for p in range(PT):
                sq = ysqb[:, p % 2, :]
                E("scalar", "square", sq, y[:, p, cc],
                  waits=(y_toks[c], PAR, ysq_tok[p % 2]), inc=bump("ss"))
                SQ = at("ss")
                mm(BANK(2 * c, 0, 1), ones_col, y[:, p, cc], p == 0,
                   p == PT - 1,
                   waits=(y_toks[c], PAR, bank_free[2 * c]) if p == 0 else (),
                   inc=p == PT - 1)
                ysq_tok[p % 2] = mm(BANK(2 * c + 1, 0, 1), ones_colb, sq,
                                    p == 0, p == PT - 1,
                                    waits=(SQ, bank_free[2 * c + 1])
                                    if p == 0 else (SQ,))
            stats_tok[c] = at("st")

        def finish(c):
            nonlocal rows_free
            cc = slice(c * CH, (c + 1) * CH)
            m, wa, wb_ = rows[0:1, 0, :], rows[0:1, 1, :], rows[0:1, 2, :]
            E("vector", "tensor_single_scalar", m, BANK(2 * c, 0, 1), INVD,
              ALU.mult, waits=(stats_tok[c], rows_free))
            E("vector", "tensor_mul", wa, m, m)
            E("vector", "scalar_tensor_tensor", wb_, BANK(2 * c + 1, 0, 1),
              INVD, wa, ALU.mult, ALU.subtract, inc=bump("sv"))
            V1 = at("sv")
            E("scalar", "activation", wa, wb_, AF.Sqrt, waits=(V1,),
              bias=epsb, scale=1.0, inc=bump("ss"))
            S1 = at("ss")
            E("vector", "reciprocal", wb_, wa, waits=(S1,))
            E("vector", "scalar_tensor_tensor", wa, m, -1.0, wb_, ALU.mult,
              ALU.mult, inc=bump("sv"))
            V2 = at("sv")
            bA = mm(BANK(2 * c), ones_row, wb_, True, True, waits=(V2,))
            bB = mm(BANK(2 * c + 1), ones_row, wa, True, True)
            rows_free = at("st")
            for p in range(PT):
                E("vector", "tensor_tensor", t1b, y[:, p, cc], BANK(2 * c),
                  ALU.mult, waits=(bB, lnout_free) if p == 0 else ())
                E("vector", "tensor_tensor", lnout[:, p, cc], t1b,
                  BANK(2 * c + 1), ALU.add)
            E("vector", "engine_nop", inc=bump("sv"))
            APPLY = at("sv")
            apply_toks.append(APPLY)
            bank_free[2 * c] = APPLY
            bank_free[2 * c + 1] = APPLY

        stats(0)
        stats(1)
        finish(0)
        stats(2)
        finish(1)
        stats(3)
        finish(2)
        finish(3)
        return apply_toks

    def _ffn_out(c, fh, chain_tok, apply_toks, w2_tok, act_tok, ytoks):
        cc = slice(c * CH, (c + 1) * CH)
        for dt in range(PT):
            bk = dt % 2
            wbuf = dt % 4
            w2ld = dma("dW2", "sync", w2e[:, wbuf], w2b_d[dt, fh],
                       waits=(W2_READY, w2_tok[wbuf]))
            if fh == 0:
                mm(BANK(bk), b2rb[0:1, dt * 128:(dt + 1) * 128],
                   ones_rowb[:, 0:CH], True, False,
                   waits=(chain_tok, bank_free[bk], PAR), inc=False)
            for k in range(16):
                w = ()
                if k == 0:
                    w = (w2ld, chain_tok, bank_free[bk])
                mm(BANK(bk), w2e[:, wbuf, k, :], act[:, fh, k, :],
                   k == 0 and fh == 1, k == 15, waits=w, inc=k == 15)
            w2_tok[wbuf] = at("st")
            OUT = at("st")
            E("vector", "tensor_tensor", y[:, dt, cc], y[:, dt, cc], BANK(bk),
              ALU.add, waits=(OUT, apply_toks[c]), inc=bump("sv"))
            bank_free[bk] = at("sv")
        if fh == 1:
            ytoks[c] = at("sv")
        act_tok[fh] = at("st")

    def emit_ffn(apply_toks, region_free):
        w1_tok = [region_free] * 4
        w2_tok = [region_free] * 4
        act_tok = [region_free, region_free]
        ch_free = {}
        hb_rot = [2, 3, 4, 5, 6, 7]
        hbi = 0
        out_jobs = []
        ytoks = [None] * NCH
        for c in range(NCH):
            cc = slice(c * CH, (c + 1) * CH)
            for fh in range(2):
                for e2 in range(8):
                    wbuf = (fh * 8 + e2) % 4
                    w1ld = dma("dW1", "sync", w1e[:, wbuf],
                               w1b_d[fh * 8 + e2],
                               waits=(W1_READY, w1_tok[wbuf]))
                    for ft in range(2):
                        fg = fh * 16 + e2 * 2 + ft
                        bk = hb_rot[hbi % 4]
                        u = chu[:, hbi % 2, :]
                        t = cht[:, hbi % 2, :]
                        r = chrb[:, hbi % 2, :]
                        for k in range(KT):
                            w = (w1ld, apply_toks[c], bank_free[bk]) \
                                if k == 0 else ()
                            mm(BANK(bk),
                               w1e[:, wbuf, k, ft * 128:(ft + 1) * 128],
                               lnout[:, k, cc], k == 0, k == KT - 1, waits=w,
                               inc=k == KT - 1)
                        HID = at("st")
                        if ft == 1:
                            w1_tok[wbuf] = HID
                        E("scalar", "activation", u, BANK(bk), AF.Identity,
                          waits=(HID, ch_free.get(hbi % 2, Z)),
                          bias=beta[:, fg:fg + 1], scale=0.1, inc=bump("ss"))
                        PA = at("ss")
                        E("vector", "tensor_scalar", t, u, INV2PI, MAGIC,
                          ALU.mult, ALU.add, waits=(PA,))
                        E("vector", "tensor_single_scalar", r, t, MAGIC,
                          ALU.subtract)
                        E("vector", "scalar_tensor_tensor", t, r, -TWOPI, u,
                          ALU.mult, ALU.add, inc=bump("sv"))
                        PR = at("sv")
                        E("scalar", "activation", r, t, AF.Sin, waits=(PR,),
                          bias=zb, scale=1.0 - 1e-6, inc=bump("ss"))
                        PD = at("ss")
                        E("vector", "scalar_tensor_tensor", t, r,
                          cg[:, fg:fg + 1], BANK(bk), ALU.mult, ALU.add,
                          waits=(PD,), inc=bump("sv"))
                        STT = at("sv")
                        bank_free[bk] = STT
                        w = (STT,) if (e2, ft) != (0, 0) else (STT, act_tok[fh])
                        E("scalar", "activation",
                          act[:, fh, e2 * 2 + ft, :], t, AF.Relu,
                          waits=w, bias=b1g[:, fg:fg + 1], scale=1.0,
                          inc=bump("ss"))
                        ch_free[hbi % 2] = at("ss")
                        hbi += 1
                out_jobs.append((c, fh, at("ss")))
                if len(out_jobs) > 1:
                    _ffn_out(*out_jobs.pop(0), apply_toks, w2_tok, act_tok,
                             ytoks)
        while out_jobs:
            _ffn_out(*out_jobs.pop(0), apply_toks, w2_tok, act_tok, ytoks)
        return ytoks, at("st"), at("st")

    def emit_attn(yn_toks, apply2_toks, region_free):
        E("vector", "memset", Vau[:, :, :, 64:65], 1.0,
          waits=(yn_toks[0], region_free), inc=bump("sv"))
        VONE = at("sv")
        vwld = dma("dQ", "sync", vwbuf, vwb_d,
                   waits=(VW_READY, region_free))
        for tt in range(H):
            for dh in range(2):
                bkp = 6 + (tt * 2 + dh) % 2
                for k in range(KT):
                    w = (vwld, yn_toks[tt // 4], bank_free[bkp]) \
                        if k == 0 else ()
                    mm(BANK(bkp), lnout[:, k, tt * 128:(tt + 1) * 128],
                       vwbuf[:, k, dh * 512:(dh + 1) * 512],
                       k == 0, k == KT - 1, waits=w, inc=k == KT - 1)
                VP = at("st")
                E("vector", "tensor_copy",
                  Vau[:, tt, dh * 8:(dh + 1) * 8, 0:64], BANK(bkp),
                  waits=(VP, VONE), inc=bump("sv"))
                bank_free[bkp] = at("sv")
        VDONE = at("st")

        S_free = [[bank_free[0], bank_free[1]], [bank_free[2], bank_free[3]]]
        E_free = [[region_free, region_free], [region_free, region_free]]
        strm_tok = [region_free, region_free]
        pv_free = [bank_free[4], bank_free[5]]
        attnTh_free = [VDONE]
        lnout_used = [VDONE]
        norm_qA = []
        norm_qB = []
        nslot = [0]
        slot_free = [Z, Z, Z, Z]
        reccol_free = [Z, Z, Z, Z]
        rcb_free = [("st", 0)] * 4
        att_ytoks = [None] * NCH

        def _emit_pv_pair(p, nkt, kt, rdy, rel, nq):
            for s2 in range(2):
                w = [rdy[s2]]
                if kt == 0:
                    w.append(pv_free[s2])
                mm(BANK(4 + s2, 0, 65, rel, rel + nq),
                   Vau[:, kt, 2 * p + s2, :],
                   Ebuf[:, 2 * s2 + kt % 2, 0:nq],
                   kt == 0, kt == nkt - 1, waits=tuple(w), inc=True)
                E_free[s2][kt % 2] = at("st")

        def _flush_A(keep=0):
            while len(norm_qA) > keep:
                CPt, base, pm, q0n, sl = norm_qA.pop(0)
                for j in range(4):
                    w = (CPt, bank_free[7]) if j == 0 else ()
                    mm(BANK(7, 0, 128, 0, 4),
                       pvsb[64:65, sl, j * 128:(j + 1) * 128],
                       oh8f[64:65, j, 0:4], j == 0, j == 3, waits=w,
                       tp=(64, 0), inc=j == 3)
                DENP = at("st")
                E("vector", "reciprocal", reccol[:, sl, :],
                  BANK(7, 0, 128, 0, 4),
                  waits=(DENP, reccol_free[sl]), inc=bump("sv"))
                bank_free[7] = at("sv")
                RECt = at("sv")
                for j in range(4):
                    E("scalar", "activation", rcb[:, sl, j, :],
                      onesb[:, 64:128], AF.Identity,
                      waits=(RECt, rcb_free[sl]) if j == 0 else (),
                      bias=reccol[:, sl, j:j + 1], scale=0.0,
                      inc=bump("ss") if j == 3 else None)
                reccol_free[sl] = at("ss")
                norm_qB.append([at("ss"), base, pm, q0n, sl])

        def _flush_B(keep=0):
            while len(norm_qB) > keep:
                RCBt, base, pm, q0n, sl = norm_qB.pop(0)
                for j2 in range(4):
                    w = (RCBt, bank_free[6]) if j2 == 0 else ()
                    mm(BANK(6, 0, 64, j2 * 128, (j2 + 1) * 128),
                       rcb[:, sl, j2, :], idmb, j2 == 0, j2 == 3,
                       waits=w, inc=j2 == 3)
                bc = at("st")
                E("vector", "tensor_tensor",
                  attnTh[base:base + 64, pm, q0n:q0n + CH],
                  pvsb[0:64, sl, :], BANK(6, 0, 64), ALU.mult,
                  waits=(bc, attnTh_free[0]), inc=bump("sv"))
                bank_free[6] = at("sv")
                rcb_free[sl] = at("st")
                slot_free[sl] = at("sv")

        def emit_pair(p):
            pb = p % 2
            qld = dma("dQ", "sync", strm[:, pb, 0], qwb_d[p],
                      waits=(QK_READY, strm_tok[pb]))
            kld = dma("dQ", "sync", strm[:, pb, 1], kwb_d[p],
                      waits=(strm_tok[pb],))
            for wq in range(2):
                for c4 in range(NCH):
                    bkp = 6 + (wq * 4 + c4) % 2
                    for k in range(KT):
                        w = (qld, kld, yn_toks[c4], bank_free[bkp]) \
                            if k == 0 else ()
                        mm(BANK(bkp), strm[:, pb, wq, k, :],
                           lnout[:, k, c4 * CH:(c4 + 1) * CH],
                           k == 0, k == KT - 1, waits=w, inc=k == KT - 1)
                    PJm = at("st")
                    E("vector", "tensor_copy",
                      qkT[:, wq, c4 * CH:(c4 + 1) * CH], BANK(bkp),
                      waits=(PJm,), inc=bump("sv"))
                    bank_free[bkp] = at("sv")
            strm_tok[pb] = at("st")
            lnout_used[0] = at("st")
            PROJ = at("sv")

            for c4 in range(NCH):
                q0 = c4 * CH
                nkt = 4 * c4 + 4
                pend = None
                for kt in range(nkt):
                    qs = max(q0, 128 * kt)
                    rel = qs - q0
                    nq = q0 + CH - qs
                    par = kt % 2
                    diag = qs == 128 * kt
                    for s2 in range(2):
                        base2 = 64 * s2
                        bqs = 2 * s2 + par
                        w = [S_free[s2][par]]
                        if kt == 0 and c4 == 0:
                            w.append(PROJ)
                        mm(BANK(bqs, 0, 128, rel, rel + nq),
                           qkT[base2:base2 + 64, 1, kt * 128:(kt + 1) * 128],
                           qkT[base2:base2 + 64, 0, qs:qs + nq],
                           True, not diag, waits=tuple(w), tp=(base2, 0),
                           inc=not diag)
                    if diag:
                        for s2 in range(2):
                            bqs = 2 * s2 + par
                            mm(BANK(bqs, 0, 128, rel, rel + 128),
                               idmb, mcmb, False, True, inc=True)
                    rdy = []
                    for s2 in range(2):
                        bqs = 2 * s2 + par
                        E("scalar", "activation", Ebuf[:, 2 * s2 + par, 0:nq],
                          BANK(bqs, 0, 128, rel, rel + nq),
                          AF.Exp, waits=(at("st"), E_free[s2][par]),
                          bias=expb, scale=1.0 / 32, inc=bump("ss"))
                        rdy.append(at("ss"))
                        S_free[s2][par] = at("ss")
                    if pend is not None:
                        _emit_pv_pair(p, nkt, *pend)
                        if kt >= 2:
                            _flush_A()
                            _flush_B(keep=1)
                    pend = (kt, rdy, rel, nq)
                _emit_pv_pair(p, nkt, *pend)
                PVt = at("st")
                for s2 in range(2):
                    sl = nslot[0] % 4
                    nslot[0] += 1
                    E("vector", "tensor_copy", pvsb[0:65, sl, :],
                      BANK(4 + s2, 0, 65), waits=(PVt, slot_free[sl]),
                      inc=bump("sv"))
                    pv_free[s2] = at("sv")
                    norm_qA.append([at("sv"), 64 * s2, p % 4, q0, sl])
            return at("sv")

        for grp in range(2):
            for p in range(4 * grp, 4 * grp + 4):
                emit_pair(p)
            _flush_A()
            _flush_B()
            ATT = at("sv")
            if DEBUG and grp == 0 and dbg_first[0]:
                dbg_first[0] = False
                dma("dA", "gpsimd",
                    dbg_ln1.rearrange("(p g) t -> p g t", g=8)[:, 0:4, :],
                    attnTh[:, :, :], waits=(ATT,))
            owv = strm[:].rearrange("p a b k m -> p (a b k m)").rearrange(
                "p (j m) -> p j m", j=4, m=1024)
            owld = dma("dQ", "sync", owv, owb_d[grp],
                       waits=(OW_READY, strm_tok[0], strm_tok[1]))
            for c4 in range(NCH):
                cc = slice(c4 * CH, (c4 + 1) * CH)
                for dt in range(PT):
                    bk = dt % 2
                    for j in range(4):
                        w = ()
                        if j == 0:
                            w = (owld, ATT, bank_free[bk]) + tuple(S_free[0])
                        mm(BANK(bk), owv[:, j, dt * 128:(dt + 1) * 128],
                           attnTh[:, j, cc], j == 0, j == 3, waits=w,
                           inc=j == 3)
                    OWm = at("st")
                    E("vector", "tensor_tensor", y[:, dt, cc], y[:, dt, cc],
                      BANK(bk), ALU.add, waits=(OWm, apply2_toks[c4]),
                      inc=bump("sv"))
                    bank_free[bk] = at("sv")
                if grp == 1:
                    att_ytoks[c4] = at("sv")
            OW = at("st")
            attnTh_free[0] = OW
            strm_tok[0] = OW
            strm_tok[1] = OW
            S_free[0] = [at("sv"), at("sv")]
        bank_free[2] = S_free[1][0]
        bank_free[3] = S_free[1][1]
        bank_free[4] = pv_free[0]
        bank_free[5] = pv_free[1]
        return att_ytoks, lnout_used[0], at("st")

    if DEBUG:
        dma("dA", "gpsimd", dbg_vg, vg, waits=(PAR,))
        pass
    y_toks = [xload] * NCH
    lnout_free = Z
    region_tok = OW_READY
    NPH = int(os.environ.get("KMK_PHASES", "6"))
    for s in range(seqs):
        ap1 = emit_ln(y_toks, lnout_free)
        f2_toks = ap1
        xn_used2 = at("st")
        if NPH >= 2:
            f1_toks, xn_used, reg1 = emit_ffn(ap1, region_tok)
            f2_toks, xn_used2, region_tok = f1_toks, xn_used, reg1
        if NPH >= 3:
            ap2 = emit_ln(f1_toks, xn_used)
            f2_toks = ap2
        if NPH >= 4:
            a_toks, yn_used, reg2 = emit_attn(ap2, ap2, reg1)
            f2_toks, xn_used2, region_tok = a_toks, yn_used, reg2
        if NPH >= 5:
            ap3 = emit_ln(a_toks, yn_used)
            f2_toks = ap3
        if NPH >= 6:
            f2_toks, xn_used2, reg3 = emit_ffn(ap3, reg2)
            region_tok = reg3
        y_toks = []
        for c in range(NCH):
            cc = slice(c * CH, (c + 1) * CH)
            stt = dma("dX", "gpsimd",
                      yt_d[s].rearrange("(pt p) t -> p pt t", p=128)[:, :, cc],
                      y[:, :, cc], waits=(f2_toks[c],))
            if s + 1 < seqs:
                y_toks.append(dma("dX", "gpsimd", y[:, :, cc],
                                  xt_d[s + 1].rearrange(
                                      "(pt p) t -> p pt t", p=128)[:, :, cc],
                                  waits=(stt,)))
        lnout_free = xn_used2
    prog["gpsimd"].append(("wait_ge", ("dX", cnt["dX"]), {}, (), None))

    blk = st.enter_context(nc.Block())

    def replay(eng_name):
        def go(eng):
            seen = {}
            for meth, args, kw, waits, inc in prog[eng_name]:
                for sname, val in waits:
                    if seen.get(sname, 0) >= val:
                        continue
                    eng.wait_ge(sems[sname], val)
                    seen[sname] = val
                if meth == "wait_ge":
                    if seen.get(args[0], 0) < args[1]:
                        eng.wait_ge(sems[args[0]], args[1])
                        seen[args[0]] = args[1]
                    continue
                ins = getattr(eng, meth)(*args, **kw)
                if inc is not None:
                    ins.then_inc(sems[inc[0]],
                                 16 if inc[0].startswith("d") else 1)
            return None
        return go

    blk.sync(replay("sync"))
    blk.gpsimd(replay("gpsimd"))
    blk.vector(replay("vector"))
    blk.scalar(replay("scalar"))
    blk.tensor(replay("tensor"))
    st.close()
    return nc


_PERM = np.arange(D).reshape(HD, H).T.ravel()


def host_prep(inputs, ncores=NCORES, seqs_per_core=4):
    x = np.asarray(inputs["L0_attn_x"])
    shared = {
        "qwp": np.ascontiguousarray(np.asarray(inputs["L0_attn_qw"])[:, _PERM]),
        "kwp": np.ascontiguousarray(np.asarray(inputs["L0_attn_kw"])[:, _PERM]),
        "vwp": np.ascontiguousarray(np.asarray(inputs["L0_attn_vw"])[:, _PERM]),
        "owp": np.ascontiguousarray(np.asarray(inputs["L0_attn_ow"])[_PERM, :]),
        "w1": np.ascontiguousarray(np.asarray(inputs["L0_ffn_W1"])),
        "w2": np.ascontiguousarray(np.asarray(inputs["L0_ffn_W2"])),
        "b1g": np.ascontiguousarray(
            np.asarray(inputs["L0_ffn_b1"]).reshape(FT, 128).T),
        "thg": np.ascontiguousarray(
            np.asarray(inputs["L0_ffn_quantum_phases"]).reshape(FT, 128).T),
        "wrg": np.ascontiguousarray(
            np.asarray(inputs["L0_ffn_quantum_weights_real"]).reshape(FT, 128).T),
        "wig": np.ascontiguousarray(
            np.asarray(inputs["L0_ffn_quantum_weights_imag"]).reshape(FT, 128).T),
        "b2g": np.ascontiguousarray(
            np.asarray(inputs["L0_ffn_b2"]).reshape(PT, 128).T),
        "tri": np.triu(np.ones((128, 128))).astype(ml_dtypes.bfloat16),
        "b2row": np.ascontiguousarray(
            np.asarray(inputs["L0_ffn_b2"]).reshape(1, D)),
        "idm": np.eye(128).astype(ml_dtypes.bfloat16),
        "mcm": np.tril(np.full((128, 128), -1e9), -1).astype(
            ml_dtypes.bfloat16),
    }
    in_maps = []
    for c in range(ncores):
        m = dict(shared)
        xs = x[c * seqs_per_core:(c + 1) * seqs_per_core]
        m["xt"] = np.ascontiguousarray(xs.transpose(0, 2, 1)).astype(
            np.float32, copy=False)
        in_maps.append(m)
    return in_maps


_CACHE = {}


def kernel(**inputs):
    if "nc" not in _CACHE:
        _CACHE["nc"] = build_nc(4)
    nc = _CACHE["nc"]
    in_maps = host_prep(inputs)
    res = run_bass_kernel_spmd(nc, in_maps, core_ids=list(range(NCORES)))
    out = np.empty((32, T, D), dtype=np.float32)
    for c in range(NCORES):
        out[c * 4:(c + 1) * 4] = np.asarray(res.results[c]["yt"]).transpose(0, 2, 1)
    return out


# version 21
# speedup vs baseline: 1.1269x; 1.1269x over previous
import os
import numpy as np
import ml_dtypes
import concourse.bass as bass
import concourse.mybir as mybir
from concourse.bass_utils import run_bass_kernel_spmd

F32 = mybir.dt.float32
BF16 = mybir.dt.bfloat16
U8 = mybir.dt.uint8
AF = mybir.ActivationFunctionType
ALU = mybir.AluOpType

NCORES = 8
T = 2048
D = 1024
FF = 4096
H = 16
HD = 64
PT = D // 128
FT = FF // 128
KT = 8
CH = 512
NCH = T // CH
MAGIC = 12582912.0
TWOPI = float(2 * np.pi)
INV2PI = float(1.0 / (2 * np.pi))
INVD = 1.0 / D
EPS = 1e-5


def _chunks512(a, b):
    out = []
    while a < b:
        e = min(b, (a // 512 + 1) * 512)
        out.append((a, e))
        a = e
    return out


def build_nc(seqs=4):
    nc = bass.Bass("TRN2", target_bir_lowering=False,
                   detect_race_conditions=os.environ.get("KMK_RACES", "") == "1")

    xt_d = nc.dram_tensor("xt", [seqs, D, T], F32, kind="ExternalInput").ap()
    qw_d = nc.dram_tensor("qwp", [D, D], F32, kind="ExternalInput").ap()
    kw_d = nc.dram_tensor("kwp", [D, D], F32, kind="ExternalInput").ap()
    vw_d = nc.dram_tensor("vwp", [D, D], F32, kind="ExternalInput").ap()
    ow_d = nc.dram_tensor("owp", [D, D], F32, kind="ExternalInput").ap()
    w1_d = nc.dram_tensor("w1", [D, FF], F32, kind="ExternalInput").ap()
    w2_d = nc.dram_tensor("w2", [FF, D], F32, kind="ExternalInput").ap()
    b1_d = nc.dram_tensor("b1g", [128, FT], F32, kind="ExternalInput").ap()
    th_d = nc.dram_tensor("thg", [128, FT], F32, kind="ExternalInput").ap()
    wr_d = nc.dram_tensor("wrg", [128, FT], F32, kind="ExternalInput").ap()
    wi_d = nc.dram_tensor("wig", [128, FT], F32, kind="ExternalInput").ap()
    b2_d = nc.dram_tensor("b2g", [128, PT], F32, kind="ExternalInput").ap()
    tri_d = nc.dram_tensor("tri", [128, 128], BF16, kind="ExternalInput").ap()
    idm_d = nc.dram_tensor("idm", [128, 128], BF16, kind="ExternalInput").ap()
    mcm_d = nc.dram_tensor("mcm", [128, 128], BF16, kind="ExternalInput").ap()
    b2r_d = nc.dram_tensor("b2row", [1, D], F32, kind="ExternalInput").ap()
    yt_d = nc.dram_tensor("yt", [seqs, D, T], F32, kind="ExternalOutput").ap()
    DEBUG = os.environ.get("KMK_DEBUG", "") == "1"
    if DEBUG:
        dbg_ln1 = nc.dram_tensor("dbg_ln1", [D, T], BF16, kind="ExternalOutput").ap()
        dbg_f1 = nc.dram_tensor("dbg_f1", [D, T], F32, kind="ExternalOutput").ap()
        dbg_att = nc.dram_tensor("dbg_att", [D, T], F32, kind="ExternalOutput").ap()
        dbg_w1 = nc.dram_tensor("dbg_w1", [D, 512], BF16, kind="ExternalOutput").ap()
        dbg_w2 = nc.dram_tensor("dbg_w2", [512, 128], BF16, kind="ExternalOutput").ap()
        dbg_act = nc.dram_tensor("dbg_act", [128, 16, 512], BF16, kind="ExternalOutput").ap()
        dbg_vg = nc.dram_tensor("dbg_vg", [128, 8, 32], F32, kind="ExternalOutput").ap()
        dbg_hid = nc.dram_tensor("dbg_hid", [128, 512], F32, kind="ExternalOutput").ap()

    w1b_d = nc.dram_tensor("w1b", [16, 128, KT, 256], BF16, kind="Internal").ap()
    w2b_d = nc.dram_tensor("w2b", [PT, 2, 128, 16, 128], BF16, kind="Internal").ap()
    qwb_d = nc.dram_tensor("qwb", [8, 128, KT, 128], BF16, kind="Internal").ap()
    kwb_d = nc.dram_tensor("kwb", [8, 128, KT, 128], BF16, kind="Internal").ap()
    vwb_d = nc.dram_tensor("vwb", [128, KT, D], BF16, kind="Internal").ap()
    owb_d = nc.dram_tensor("owb", [2, 128, 4, D], BF16, kind="Internal").ap()

    from contextlib import ExitStack
    st = ExitStack()
    nc._exit_stack = st

    y = st.enter_context(nc.sbuf_tensor("y", [128, PT, T], F32))
    lnout = st.enter_context(nc.sbuf_tensor("lnout", [128, PT, T], BF16))
    POOLB = 95 * 1024
    pool_t = st.enter_context(nc.sbuf_tensor("pool", [128, POOLB], U8))
    pool = pool_t[:]
    _off = [0]

    def carve(nbytes, dtype, shape=None, off=None):
        o = (_off[0] + 63) // 64 * 64 if off is None else off
        end = o + nbytes
        if off is None:
            _off[0] = end
        assert end <= POOLB, f"pool overflow {end}"
        v = pool[:, o:o + nbytes].bitcast(dtype)
        if shape is not None:
            names = [chr(97 + i) for i in range(len(shape))]
            v = v.rearrange("p (" + " ".join(names) + ") -> p " + " ".join(names),
                            **{n: s for n, s in zip(names, shape)})
        return v, o

    rows, rows_off = carve(2048 * 4, F32, [4, 512])
    b2rb, _ = carve(1024 * 2, BF16)
    rec, _ = carve(2048 * 4, F32, [2, 1024], off=rows_off)
    lnrow, _ = carve(2 * 2 * 512 * 2, BF16, [2, 2, 512], off=rows_off)
    lnbrow, _ = carve(2 * 2 * 512 * 2, BF16, [2, 2, 512], off=rows_off + 4096)
    lncol, _ = carve(2 * 32 * 4, F32, [2, 32])
    lncolb, _ = carve(2 * 8 * 2, BF16, [2, 8])
    vg, _ = carve(8 * 32 * 4, F32, [8, 32])
    b2v, _ = carve(2 * 8 * 4, F32, [2, 8])
    sc1, _ = carve(8 * 4, F32)
    onesr, _ = carve(128 * 4, F32)
    onesb, _ = carve(512 * 2, BF16)
    trib, _ = carve(128 * 2, BF16)
    idmb, _ = carve(128 * 2, BF16)
    mcmb, _ = carve(128 * 2, BF16)
    emask, _ = carve(2 * 128 * 2, BF16, [2, 128])
    ysqb, _ = carve(2 * 512 * 2, BF16, [2, 512])
    reccol, _ = carve(4 * 4 * 4, F32, [4, 4])
    rcb, _ = carve(4 * 4 * 64 * 2, BF16, [4, 4, 64])
    oh8f, _ = carve(8 * 8 * 2, BF16, [8, 8])
    sh0 = (_off[0] + 63) // 64 * 64
    t1b, _ = carve(512 * 4, F32, off=sh0 + 4096)
    chu, _ = carve(2 * 512 * 4, F32, [2, 512], off=sh0)
    cht, _ = carve(2 * 512 * 4, F32, [2, 512], off=sh0 + 4096)
    chrb, _ = carve(2 * 512 * 4, F32, [2, 512], off=sh0 + 8192)
    pvsb, _ = carve(4 * 512 * 2, BF16, [4, 512], off=sh0)
    _off[0] = sh0 + 12288

    ph0 = (_off[0] + 63) // 64 * 64
    _off[0] = ph0
    act, _ = carve(2 * 16 * CH * 2, BF16, [2, 16, CH])
    w1e, _ = carve(4 * KT * 256 * 2, BF16, [4, KT, 256])
    w2e, _ = carve(4 * 16 * 128 * 2, BF16, [4, 16, 128])
    ffn_end = _off[0]
    Ebuf, _ = carve(4 * 512 * 2, BF16, [4, 512], off=sh0 + 8192)
    _off[0] = ph0
    Vau, _ = carve(H * H * 65 * 2, BF16, [H, H, 65])
    qkT, _ = carve(2 * T * 2, BF16, [2, T])
    attnTh, at_off = carve(4 * T * 2, BF16, [4, T])
    strm, _ = carve(2 * 2 * KT * 128 * 2, BF16, [2, 2, KT, 128])
    attn_end = _off[0]
    vwbuf = pool[:, at_off:at_off + 4 * T * 2].bitcast(BF16).rearrange(
        "p (k m) -> p k m", k=KT, m=D)
    stgf, _ = carve(2 * 4096 * 4, F32, off=ph0)
    stgb, _ = carve(2 * 4096 * 2, BF16, off=ph0 + 32768)
    _off[0] = max(ffn_end, attn_end)

    pall = st.enter_context(nc.psum_tensor("pall", [128, 4096], F32))

    def BANK(b, r0=0, r1=128, c0=0, c1=512):
        return pall[r0:r1, 512 * b + c0:512 * b + c1]

    sems = {n: st.enter_context(nc.semaphore(n)) for n in
            ("dP", "dPS", "dW1", "dW2", "dQ", "dX", "dA", "st", "sv", "ss")}
    cnt = {n: 0 for n in sems}
    prog = {k: [] for k in ("sync", "gpsimd", "vector", "scalar", "tensor")}

    def E(eng, meth, *args, waits=(), inc=None, **kw):
        ws = tuple((s, v) for (s, v) in waits if v > 0)
        prog[eng].append((meth, args, kw, ws, inc))

    def bump(sem, n=1):
        cnt[sem] += n
        return (sem, cnt[sem])

    def at(sem):
        return (sem, cnt[sem])

    DMA_SEMS = ("dP", "dW1", "dW2", "dQ", "dX", "dA")

    def dma(stream, eng, out, in_, waits=()):
        E(eng, "dma_start", waits=waits, inc=bump(stream, 16), out=out, in_=in_)
        return at(stream)

    def mm(out, lhsT, rhs, start, stop, waits=(), tp=None, inc=True):
        kw = {"start": start, "stop": stop}
        if tp is not None:
            kw["tile_position"] = tp
        E("tensor", "matmul", out, lhsT, rhs, waits=waits,
          inc=(bump("st") if inc else None), **kw)
        return at("st")

    Z = ("sv", 0)

    pieces = []
    w1r = w1_d.rearrange("(kt p) f -> kt p f", p=128)
    for k in range(KT):
        dsts = [(w1b_d[pc, :, k, :], pc * 256, (pc + 1) * 256)
                for pc in range(16)]
        pieces.append((w1r[k], dsts, FF, 1.0, "w1"))
    w2r = w2_d.rearrange("(ft p) d -> ft p d", p=128)
    for f in range(FT):
        dsts = [(w2b_d[dt, f // 16, :, f % 16, :], dt * 128, (dt + 1) * 128)
                for dt in range(PT)]
        pieces.append((w2r[f], dsts, D, 0.5, "w2"))
    for nm, src_, dst in (("qk", qw_d, qwb_d), ("qk", kw_d, kwb_d)):
        sr = src_.rearrange("(kt p) d -> kt p d", p=128)
        for k in range(KT):
            dsts = [(dst[pr, :, k, :], pr * 128, (pr + 1) * 128)
                    for pr in range(8)]
            pieces.append((sr[k], dsts, D, 1.0, nm))
    vr = vw_d.rearrange("(kt p) d -> kt p d", p=128)
    for k in range(KT):
        pieces.append((vr[k], [(vwb_d[:, k, :], 0, D)], D, 1.0, "vw"))
    orr = ow_d.rearrange("(kt p) d -> kt p d", p=128)
    for k in range(KT):
        pieces.append((orr[k], [(owb_d[k // 4, :, k % 4, :], 0, D)], D, 1.0,
                       "ow"))

    stg_conv = [Z, Z]
    ready_tok = {}
    for i, (src_, dsts, w, scale, grp) in enumerate(pieces):
        b = i % 2
        o = b * 4096
        ld = dma("dP", "sync", stgf[:, o:o + w], src_, waits=(stg_conv[b],))
        all_stores = at("dPS")
        if scale == 1.0:
            E("vector", "tensor_copy", stgb[:, o:o + w], stgf[:, o:o + w],
              waits=(ld, all_stores), inc=bump("sv"))
        else:
            E("vector", "tensor_single_scalar", stgb[:, o:o + w],
              stgf[:, o:o + w], scale, ALU.mult,
              waits=(ld, all_stores), inc=bump("sv"))
        stg_conv[b] = at("sv")
        for dst_ap, c0, c1 in dsts:
            dma("dPS", "sync", dst_ap, stgb[:, o + c0:o + c1],
                waits=(stg_conv[b],))
        ready_tok[grp] = at("dPS")
    W1_READY = ready_tok["w1"]
    W2_READY = ready_tok["w2"]
    QK_READY = ready_tok["qk"]
    VW_READY = ready_tok["vw"]
    OW_READY = ready_tok["ow"]

    xload = dma("dX", "gpsimd", y[:],
                xt_d[0].rearrange("(pt p) t -> p pt t", p=128))
    for dst, src in ((vg[:, 0, :], b1_d), (vg[:, 1, :], th_d),
                     (vg[:, 2, :], wr_d), (vg[:, 3, :], wi_d),
                     (b2v[:, 0, :], b2_d), (trib, tri_d),
                     (idmb, idm_d), (mcmb, mcm_d),
                     (rec[0:1, 0, :], b2r_d)):
        dma("dA", "gpsimd", dst, src)
    AUX = at("dA")

    E("vector", "memset", sc1[:, 0:1], -3.0, waits=(AUX,))
    E("vector", "memset", sc1[:, 1:2], 0.0)
    E("vector", "memset", sc1[:, 2:3], EPS)
    E("vector", "memset", sc1[:, 3:4], MAGIC)
    E("vector", "memset", sc1[:, 4:5], -MAGIC)
    E("vector", "memset", onesr[:, 0:1], 1.0)
    E("vector", "memset", onesr[0:1, :], 1.0)
    E("vector", "memset", onesr[64:65, 0:64], 1.0)
    E("vector", "memset", onesb[:, 0:1], 1.0)
    E("vector", "memset", onesb[0:1, :], 1.0)
    E("vector", "memset", onesb[:, 64:128], 1.0)
    E("vector", "memset", oh8f[:], 0.0)
    for _j in range(8):
        E("vector", "memset", oh8f[64:65, _j, _j:_j + 1], 1.0)
        E("vector", "memset", oh8f[0:1, _j, _j:_j + 1], 1.0)
    E("vector", "tensor_single_scalar", b2rb[0:1, :], rec[0:1, 0, :], 0.5,
      ALU.mult)
    expb = sc1[:, 0:1]
    zb = sc1[:, 1:2]
    epsb = sc1[0:1, 2:3]
    magb = sc1[:, 3:4]
    nmagb = sc1[:, 4:5]
    ones_colb = onesb[:, 0:1]
    ones_rowb = onesb[0:1, :]
    ones_col = onesr[:, 0:1]
    ones_row = onesr[0:1, 0:128]
    ones64 = onesr[64:65, 0:64]
    b1g, thg, wrg, wig = (vg[:, i, :] for i in range(4))
    beta, cg, tm1, tm2 = (vg[:, i, :] for i in range(4, 8))
    def vstep(meth, *args, **kw):
        E("vector", meth, *args, waits=(at("sv"),), inc=bump("sv"), **kw)

    E("vector", "scalar_tensor_tensor", beta, b1g, 0.1, thg, ALU.mult, ALU.add)
    vstep("tensor_scalar", tm1, thg, INV2PI, MAGIC, ALU.mult, ALU.add)
    vstep("tensor_single_scalar", tm2, tm1, MAGIC, ALU.subtract)
    vstep("scalar_tensor_tensor", tm1, tm2, -TWOPI, thg, ALU.mult, ALU.add)
    VD = at("sv")
    E("scalar", "activation", tm2, tm1, AF.Sin, waits=(VD,), bias=zb,
      scale=1.0 - 1e-6)
    E("scalar", "activation", cg, tm1, AF.Sin, bias=zb, scale=0.5 * (1 - 1e-6),
      inc=bump("ss"))
    SD = at("ss")
    E("vector", "tensor_mul", tm1, cg, cg, waits=(SD,), inc=bump("sv"))
    vstep("tensor_scalar", cg, tm1, -2.0, 1.0, ALU.mult, ALU.add)
    vstep("tensor_mul", tm1, cg, wrg)
    vstep("tensor_mul", tm2, tm2, wig)
    vstep("tensor_add", tm1, tm1, tm2)
    vstep("scalar_tensor_tensor", cg, tm1, 0.1, wig, ALU.mult, ALU.mult)
    vstep("engine_nop")
    PAR = at("sv")

    bank_free = [Z] * 8
    dbg_first = [True]
    dbg_hid_first = [True]
    ZT = ("st", 0)
    ln_free = {"row": [ZT, ZT], "col": [ZT, ZT], "brow": [ZT, ZT]}
    zb0 = sc1[0:1, 1:2]
    epsc = sc1[:, 2:3]

    def emit_ln(y_toks, lnout_free):
        ysq_tok = [("st", 0)] * 2
        stats_tok = [None] * NCH
        apply_toks = []

        def stats(c):
            cc = slice(c * CH, (c + 1) * CH)
            for p in range(PT):
                sq = ysqb[:, p % 2, :]
                E("scalar", "square", sq, y[:, p, cc],
                  waits=(y_toks[c], PAR, ysq_tok[p % 2]), inc=bump("ss"))
                SQ = at("ss")
                mm(BANK(2 * c, 0, 1), ones_col, y[:, p, cc], p == 0,
                   p == PT - 1,
                   waits=(y_toks[c], PAR, bank_free[2 * c]) if p == 0 else (),
                   inc=p == PT - 1)
                ysq_tok[p % 2] = mm(BANK(2 * c + 1, 0, 1), ones_colb, sq,
                                    p == 0, p == PT - 1,
                                    waits=(SQ, bank_free[2 * c + 1])
                                    if p == 0 else (SQ,))
            stats_tok[c] = at("st")

        def finish(c):
            sl = c % 2
            cc = slice(c * CH, (c + 1) * CH)
            E("scalar", "activation", lnrow[0:1, sl, 0, :], BANK(2 * c, 0, 1),
              AF.Identity, waits=(stats_tok[c], ln_free["row"][sl]),
              bias=zb0, scale=1.0)
            E("scalar", "activation", lnrow[0:1, sl, 1, :],
              BANK(2 * c + 1, 0, 1), AF.Identity, bias=zb0, scale=1.0,
              inc=bump("ss"))
            CPs = at("ss")
            for j in range(8):
                w = (CPs,) if j == 0 else ()
                mm(BANK(2 * c, 0, 128, 8, 16),
                   lnrow[0:1, sl, j // 4, (j % 4) * 128:(j % 4 + 1) * 128],
                   oh8f[0:1, j, :], j == 0, j == 7, waits=w, inc=j == 7)
            TRt = at("st")
            ln_free["row"][sl] = TRt
            tmp8 = lncol[:, sl, 0:8]
            m4 = lncol[:, sl, 0:4]
            s4 = lncol[:, sl, 4:8]
            m2 = lncol[:, sl, 8:12]
            vv = lncol[:, sl, 12:16]
            sd = lncol[:, sl, 16:20]
            rstd_f = lncol[:, sl, 20:24]
            ma_f = lncol[:, sl, 24:28]
            rstd = lncolb[:, sl, 0:4]
            ma = lncolb[:, sl, 4:8]
            E("vector", "tensor_single_scalar", tmp8,
              BANK(2 * c, 0, 128, 8, 16), INVD, ALU.mult,
              waits=(TRt, ln_free["col"][sl]), inc=bump("sv"))
            vstep("tensor_mul", m2, m4, m4)
            vstep("tensor_tensor", vv, s4, m2, ALU.subtract)
            V1 = at("sv")
            E("scalar", "activation", sd, vv, AF.Sqrt, waits=(V1,),
              bias=epsc, scale=1.0, inc=bump("ss"))
            S1 = at("ss")
            E("vector", "reciprocal", rstd_f, sd, waits=(S1,), inc=bump("sv"))
            vstep("scalar_tensor_tensor", ma_f, m4, -1.0, rstd_f, ALU.mult,
                  ALU.mult)
            vstep("tensor_copy", lncolb[:, sl, 0:8], lncol[:, sl, 20:28])
            V2 = at("sv")
            for j in range(4):
                w = (V2,) if j == 0 else ()
                mm(BANK(2 * c + 1, 0, 1, j * 128, (j + 1) * 128),
                   rstd[:, j:j + 1], idmb, j == 0, j == 3, waits=w,
                   inc=j == 3)
            for j in range(4):
                mm(BANK(2 * c, 0, 1, j * 128, (j + 1) * 128),
                   ma[:, j:j + 1], idmb, j == 0, j == 3, inc=j == 3)
            BTt = at("st")
            ln_free["col"][sl] = BTt
            E("scalar", "activation", lnbrow[0:1, sl, 0, :],
              BANK(2 * c + 1, 0, 1), AF.Identity,
              waits=(BTt, ln_free["brow"][sl]), bias=zb0, scale=1.0)
            E("scalar", "activation", lnbrow[0:1, sl, 1, :], BANK(2 * c, 0, 1),
              AF.Identity, bias=zb0, scale=1.0, inc=bump("ss"))
            CPb = at("ss")
            bB = mm(BANK(2 * c), onesb[0:1, 0:128], lnbrow[0:1, sl, 0, :],
                    True, True, waits=(CPb,))
            bA = mm(BANK(2 * c + 1), onesb[0:1, 0:128], lnbrow[0:1, sl, 1, :],
                    True, True)
            ln_free["brow"][sl] = at("st")
            for p in range(PT):
                E("vector", "tensor_tensor", t1b, y[:, p, cc], BANK(2 * c),
                  ALU.mult, waits=(bA, lnout_free) if p == 0 else ())
                E("vector", "tensor_tensor", lnout[:, p, cc], t1b,
                  BANK(2 * c + 1), ALU.add)
            E("vector", "engine_nop", inc=bump("sv"))
            APPLY = at("sv")
            apply_toks.append(APPLY)
            bank_free[2 * c] = APPLY
            bank_free[2 * c + 1] = APPLY

        stats(0)
        stats(1)
        finish(0)
        stats(2)
        finish(1)
        stats(3)
        finish(2)
        finish(3)
        return apply_toks

    def _ffn_out(c, fh, chain_tok, apply_toks, w2_tok, act_tok, ytoks):
        cc = slice(c * CH, (c + 1) * CH)
        for dt in range(PT):
            bk = dt % 2
            wbuf = dt % 4
            w2ld = dma("dW2", "sync", w2e[:, wbuf], w2b_d[dt, fh],
                       waits=(W2_READY, w2_tok[wbuf]))
            if fh == 0:
                mm(BANK(bk), b2rb[0:1, dt * 128:(dt + 1) * 128],
                   ones_rowb[:, 0:CH], True, False,
                   waits=(chain_tok, bank_free[bk], PAR), inc=False)
            for k in range(16):
                w = ()
                if k == 0:
                    w = (w2ld, chain_tok, bank_free[bk])
                mm(BANK(bk), w2e[:, wbuf, k, :], act[:, fh, k, :],
                   k == 0 and fh == 1, k == 15, waits=w, inc=k == 15)
            w2_tok[wbuf] = at("st")
            OUT = at("st")
            E("vector", "tensor_tensor", y[:, dt, cc], y[:, dt, cc], BANK(bk),
              ALU.add, waits=(OUT, apply_toks[c]), inc=bump("sv"))
            bank_free[bk] = at("sv")
        if fh == 1:
            ytoks[c] = at("sv")
        act_tok[fh] = at("st")

    def emit_ffn(apply_toks, region_free):
        w1_tok = [region_free] * 4
        w2_tok = [region_free] * 4
        act_tok = [region_free, region_free]
        ch_free = {}
        hb_rot = [2, 3, 4, 5, 6, 7]
        hbi = 0
        out_jobs = []
        ytoks = [None] * NCH
        for c in range(NCH):
            cc = slice(c * CH, (c + 1) * CH)
            for fh in range(2):
                for e2 in range(8):
                    wbuf = (fh * 8 + e2) % 4
                    w1ld = dma("dW1", "sync", w1e[:, wbuf],
                               w1b_d[fh * 8 + e2],
                               waits=(W1_READY, w1_tok[wbuf]))
                    for ft in range(2):
                        fg = fh * 16 + e2 * 2 + ft
                        bk = hb_rot[hbi % 4]
                        u = chu[:, hbi % 2, :]
                        t = cht[:, hbi % 2, :]
                        r = chrb[:, hbi % 2, :]
                        for k in range(KT):
                            w = (w1ld, apply_toks[c], bank_free[bk]) \
                                if k == 0 else ()
                            mm(BANK(bk),
                               w1e[:, wbuf, k, ft * 128:(ft + 1) * 128],
                               lnout[:, k, cc], k == 0, k == KT - 1, waits=w,
                               inc=k == KT - 1)
                        HID = at("st")
                        if ft == 1:
                            w1_tok[wbuf] = HID
                        E("scalar", "activation", u, BANK(bk), AF.Identity,
                          waits=(HID, ch_free.get(hbi % 2, Z)),
                          bias=beta[:, fg:fg + 1], scale=0.1, inc=bump("ss"))
                        PA = at("ss")
                        E("vector", "tensor_scalar", t, u, INV2PI, MAGIC,
                          ALU.mult, ALU.add, waits=(PA,))
                        E("vector", "tensor_single_scalar", r, t, MAGIC,
                          ALU.subtract)
                        E("vector", "scalar_tensor_tensor", t, r, -TWOPI, u,
                          ALU.mult, ALU.add, inc=bump("sv"))
                        PR = at("sv")
                        E("scalar", "activation", r, t, AF.Sin, waits=(PR,),
                          bias=zb, scale=1.0 - 1e-6, inc=bump("ss"))
                        PD = at("ss")
                        E("vector", "scalar_tensor_tensor", t, r,
                          cg[:, fg:fg + 1], BANK(bk), ALU.mult, ALU.add,
                          waits=(PD,), inc=bump("sv"))
                        bank_free[bk] = at("sv")
                        w = () if (e2, ft) != (0, 0) else (act_tok[fh],)
                        E("vector", "tensor_scalar",
                          act[:, fh, e2 * 2 + ft, :], t, b1g[:, fg:fg + 1],
                          0.0, ALU.add, ALU.max, waits=w, inc=bump("sv"))
                        ch_free[hbi % 2] = at("sv")
                        hbi += 1
                out_jobs.append((c, fh, at("sv")))
                if len(out_jobs) > 1:
                    _ffn_out(*out_jobs.pop(0), apply_toks, w2_tok, act_tok,
                             ytoks)
        while out_jobs:
            _ffn_out(*out_jobs.pop(0), apply_toks, w2_tok, act_tok, ytoks)
        return ytoks, at("st"), at("st")

    def emit_attn(yn_toks, apply2_toks, region_free):
        E("vector", "memset", Vau[:, :, :, 64:65], 1.0,
          waits=(yn_toks[0], region_free), inc=bump("sv"))
        VONE = at("sv")
        vwld = dma("dQ", "sync", vwbuf, vwb_d,
                   waits=(VW_READY, region_free))
        for tt in range(H):
            for dh in range(2):
                bkp = 6 + (tt * 2 + dh) % 2
                for k in range(KT):
                    w = (vwld, yn_toks[tt // 4], bank_free[bkp]) \
                        if k == 0 else ()
                    mm(BANK(bkp), lnout[:, k, tt * 128:(tt + 1) * 128],
                       vwbuf[:, k, dh * 512:(dh + 1) * 512],
                       k == 0, k == KT - 1, waits=w, inc=k == KT - 1)
                VP = at("st")
                E("vector", "tensor_copy",
                  Vau[:, tt, dh * 8:(dh + 1) * 8, 0:64], BANK(bkp),
                  waits=(VP, VONE), inc=bump("sv"))
                bank_free[bkp] = at("sv")
        VDONE = at("st")

        S_free = [[bank_free[0], bank_free[1]], [bank_free[2], bank_free[3]]]
        E_free = [[region_free, region_free], [region_free, region_free]]
        strm_tok = [region_free, region_free]
        pv_free = [bank_free[4], bank_free[5]]
        attnTh_free = [VDONE]
        lnout_used = [VDONE]
        norm_qA = []
        norm_qB = []
        nslot = [0]
        slot_free = [Z, Z, Z, Z]
        reccol_free = [Z, Z, Z, Z]
        rcb_free = [("st", 0)] * 4
        att_ytoks = [None] * NCH

        def _emit_pv_pair(p, nkt, kt, rdy, rel, nq):
            for s2 in range(2):
                w = [rdy[s2]]
                if kt == 0:
                    w.append(pv_free[s2])
                mm(BANK(4 + s2, 0, 65, rel, rel + nq),
                   Vau[:, kt, 2 * p + s2, :],
                   Ebuf[:, 2 * s2 + kt % 2, 0:nq],
                   kt == 0, kt == nkt - 1, waits=tuple(w), inc=True)
                E_free[s2][kt % 2] = at("st")

        def _flush_A(keep=0):
            while len(norm_qA) > keep:
                CPt, base, pm, q0n, sl = norm_qA.pop(0)
                for j in range(4):
                    w = (CPt, bank_free[7]) if j == 0 else ()
                    mm(BANK(7, 0, 128, 0, 4),
                       pvsb[64:65, sl, j * 128:(j + 1) * 128],
                       oh8f[64:65, j, 0:4], j == 0, j == 3, waits=w,
                       tp=(64, 0), inc=j == 3)
                DENP = at("st")
                E("vector", "reciprocal", reccol[:, sl, :],
                  BANK(7, 0, 128, 0, 4),
                  waits=(DENP, reccol_free[sl]), inc=bump("sv"))
                bank_free[7] = at("sv")
                RECt = at("sv")
                for j in range(4):
                    E("scalar", "activation", rcb[:, sl, j, :],
                      onesb[:, 64:128], AF.Identity,
                      waits=(RECt, rcb_free[sl]) if j == 0 else (),
                      bias=reccol[:, sl, j:j + 1], scale=0.0,
                      inc=bump("ss") if j == 3 else None)
                reccol_free[sl] = at("ss")
                norm_qB.append([at("ss"), base, pm, q0n, sl])

        def _flush_B(keep=0):
            while len(norm_qB) > keep:
                RCBt, base, pm, q0n, sl = norm_qB.pop(0)
                for j2 in range(4):
                    w = (RCBt, bank_free[6]) if j2 == 0 else ()
                    mm(BANK(6, 0, 64, j2 * 128, (j2 + 1) * 128),
                       rcb[:, sl, j2, :], idmb, j2 == 0, j2 == 3,
                       waits=w, inc=j2 == 3)
                bc = at("st")
                E("vector", "tensor_tensor",
                  attnTh[base:base + 64, pm, q0n:q0n + CH],
                  pvsb[0:64, sl, :], BANK(6, 0, 64), ALU.mult,
                  waits=(bc, attnTh_free[0]), inc=bump("sv"))
                bank_free[6] = at("sv")
                rcb_free[sl] = at("st")
                slot_free[sl] = at("sv")

        def emit_pair(p):
            pb = p % 2
            qld = dma("dQ", "sync", strm[:, pb, 0], qwb_d[p],
                      waits=(QK_READY, strm_tok[pb]))
            kld = dma("dQ", "sync", strm[:, pb, 1], kwb_d[p],
                      waits=(strm_tok[pb],))
            for wq in range(2):
                for c4 in range(NCH):
                    bkp = 6 + (wq * 4 + c4) % 2
                    for k in range(KT):
                        w = (qld, kld, yn_toks[c4], bank_free[bkp]) \
                            if k == 0 else ()
                        mm(BANK(bkp), strm[:, pb, wq, k, :],
                           lnout[:, k, c4 * CH:(c4 + 1) * CH],
                           k == 0, k == KT - 1, waits=w, inc=k == KT - 1)
                    PJm = at("st")
                    E("vector", "tensor_copy",
                      qkT[:, wq, c4 * CH:(c4 + 1) * CH], BANK(bkp),
                      waits=(PJm,), inc=bump("sv"))
                    bank_free[bkp] = at("sv")
            strm_tok[pb] = at("st")
            lnout_used[0] = at("st")
            PROJ = at("sv")

            for c4 in range(NCH):
                q0 = c4 * CH
                nkt = 4 * c4 + 4
                pend = None
                for kt in range(nkt):
                    qs = max(q0, 128 * kt)
                    rel = qs - q0
                    nq = q0 + CH - qs
                    par = kt % 2
                    diag = qs == 128 * kt
                    for s2 in range(2):
                        base2 = 64 * s2
                        bqs = 2 * s2 + par
                        w = [S_free[s2][par]]
                        if kt == 0 and c4 == 0:
                            w.append(PROJ)
                        mm(BANK(bqs, 0, 128, rel, rel + nq),
                           qkT[base2:base2 + 64, 1, kt * 128:(kt + 1) * 128],
                           qkT[base2:base2 + 64, 0, qs:qs + nq],
                           True, not diag, waits=tuple(w), tp=(base2, 0),
                           inc=not diag)
                    if diag:
                        for s2 in range(2):
                            bqs = 2 * s2 + par
                            mm(BANK(bqs, 0, 128, rel, rel + 128),
                               idmb, mcmb, False, True, inc=True)
                    rdy = []
                    for s2 in range(2):
                        bqs = 2 * s2 + par
                        E("scalar", "activation", Ebuf[:, 2 * s2 + par, 0:nq],
                          BANK(bqs, 0, 128, rel, rel + nq),
                          AF.Exp, waits=(at("st"), E_free[s2][par]),
                          bias=expb, scale=1.0 / 32, inc=bump("ss"))
                        rdy.append(at("ss"))
                        S_free[s2][par] = at("ss")
                    if pend is not None:
                        _emit_pv_pair(p, nkt, *pend)
                        if kt >= 2:
                            _flush_A()
                            _flush_B(keep=1)
                    pend = (kt, rdy, rel, nq)
                _emit_pv_pair(p, nkt, *pend)
                PVt = at("st")
                for s2 in range(2):
                    sl = nslot[0] % 4
                    nslot[0] += 1
                    E("vector", "tensor_copy", pvsb[0:65, sl, :],
                      BANK(4 + s2, 0, 65), waits=(PVt, slot_free[sl]),
                      inc=bump("sv"))
                    pv_free[s2] = at("sv")
                    norm_qA.append([at("sv"), 64 * s2, p % 4, q0, sl])
            return at("sv")

        for grp in range(2):
            for p in range(4 * grp, 4 * grp + 4):
                emit_pair(p)
            _flush_A()
            _flush_B()
            ATT = at("sv")
            if DEBUG and grp == 0 and dbg_first[0]:
                dbg_first[0] = False
                dma("dA", "gpsimd",
                    dbg_ln1.rearrange("(p g) t -> p g t", g=8)[:, 0:4, :],
                    attnTh[:, :, :], waits=(ATT,))
            owv = strm[:].rearrange("p a b k m -> p (a b k m)").rearrange(
                "p (j m) -> p j m", j=4, m=1024)
            owld = dma("dQ", "sync", owv, owb_d[grp],
                       waits=(OW_READY, strm_tok[0], strm_tok[1]))
            for c4 in range(NCH):
                cc = slice(c4 * CH, (c4 + 1) * CH)
                for dt in range(PT):
                    bk = dt % 2
                    for j in range(4):
                        w = ()
                        if j == 0:
                            w = (owld, ATT, bank_free[bk]) + tuple(S_free[0])
                        mm(BANK(bk), owv[:, j, dt * 128:(dt + 1) * 128],
                           attnTh[:, j, cc], j == 0, j == 3, waits=w,
                           inc=j == 3)
                    OWm = at("st")
                    E("vector", "tensor_tensor", y[:, dt, cc], y[:, dt, cc],
                      BANK(bk), ALU.add, waits=(OWm, apply2_toks[c4]),
                      inc=bump("sv"))
                    bank_free[bk] = at("sv")
                if grp == 1:
                    att_ytoks[c4] = at("sv")
            OW = at("st")
            attnTh_free[0] = OW
            strm_tok[0] = OW
            strm_tok[1] = OW
            S_free[0] = [at("sv"), at("sv")]
        bank_free[2] = S_free[1][0]
        bank_free[3] = S_free[1][1]
        bank_free[4] = pv_free[0]
        bank_free[5] = pv_free[1]
        return att_ytoks, lnout_used[0], at("st")

    if DEBUG:
        dma("dA", "gpsimd", dbg_vg, vg, waits=(PAR,))
        pass
    y_toks = [xload] * NCH
    lnout_free = Z
    region_tok = OW_READY
    NPH = int(os.environ.get("KMK_PHASES", "6"))
    for s in range(seqs):
        ap1 = emit_ln(y_toks, lnout_free)
        f2_toks = ap1
        xn_used2 = at("st")
        if NPH >= 2:
            f1_toks, xn_used, reg1 = emit_ffn(ap1, region_tok)
            f2_toks, xn_used2, region_tok = f1_toks, xn_used, reg1
        if NPH >= 3:
            ap2 = emit_ln(f1_toks, xn_used)
            f2_toks = ap2
        if NPH >= 4:
            a_toks, yn_used, reg2 = emit_attn(ap2, ap2, reg1)
            f2_toks, xn_used2, region_tok = a_toks, yn_used, reg2
        if NPH >= 5:
            ap3 = emit_ln(a_toks, yn_used)
            f2_toks = ap3
        if NPH >= 6:
            f2_toks, xn_used2, reg3 = emit_ffn(ap3, reg2)
            region_tok = reg3
        y_toks = []
        for c in range(NCH):
            cc = slice(c * CH, (c + 1) * CH)
            stt = dma("dX", "gpsimd",
                      yt_d[s].rearrange("(pt p) t -> p pt t", p=128)[:, :, cc],
                      y[:, :, cc], waits=(f2_toks[c],))
            if s + 1 < seqs:
                y_toks.append(dma("dX", "gpsimd", y[:, :, cc],
                                  xt_d[s + 1].rearrange(
                                      "(pt p) t -> p pt t", p=128)[:, :, cc],
                                  waits=(stt,)))
        lnout_free = xn_used2
    prog["gpsimd"].append(("wait_ge", ("dX", cnt["dX"]), {}, (), None))

    blk = st.enter_context(nc.Block())

    def replay(eng_name):
        def go(eng):
            seen = {}
            for meth, args, kw, waits, inc in prog[eng_name]:
                for sname, val in waits:
                    if seen.get(sname, 0) >= val:
                        continue
                    eng.wait_ge(sems[sname], val)
                    seen[sname] = val
                if meth == "wait_ge":
                    if seen.get(args[0], 0) < args[1]:
                        eng.wait_ge(sems[args[0]], args[1])
                        seen[args[0]] = args[1]
                    continue
                ins = getattr(eng, meth)(*args, **kw)
                if inc is not None:
                    ins.then_inc(sems[inc[0]],
                                 16 if inc[0].startswith("d") else 1)
            return None
        return go

    blk.sync(replay("sync"))
    blk.gpsimd(replay("gpsimd"))
    blk.vector(replay("vector"))
    blk.scalar(replay("scalar"))
    blk.tensor(replay("tensor"))
    st.close()
    return nc


_PERM = np.arange(D).reshape(HD, H).T.ravel()


def host_prep(inputs, ncores=NCORES, seqs_per_core=4):
    x = np.asarray(inputs["L0_attn_x"])
    shared = {
        "qwp": np.ascontiguousarray(np.asarray(inputs["L0_attn_qw"])[:, _PERM]),
        "kwp": np.ascontiguousarray(np.asarray(inputs["L0_attn_kw"])[:, _PERM]),
        "vwp": np.ascontiguousarray(np.asarray(inputs["L0_attn_vw"])[:, _PERM]),
        "owp": np.ascontiguousarray(np.asarray(inputs["L0_attn_ow"])[_PERM, :]),
        "w1": np.ascontiguousarray(np.asarray(inputs["L0_ffn_W1"])),
        "w2": np.ascontiguousarray(np.asarray(inputs["L0_ffn_W2"])),
        "b1g": np.ascontiguousarray(
            np.asarray(inputs["L0_ffn_b1"]).reshape(FT, 128).T),
        "thg": np.ascontiguousarray(
            np.asarray(inputs["L0_ffn_quantum_phases"]).reshape(FT, 128).T),
        "wrg": np.ascontiguousarray(
            np.asarray(inputs["L0_ffn_quantum_weights_real"]).reshape(FT, 128).T),
        "wig": np.ascontiguousarray(
            np.asarray(inputs["L0_ffn_quantum_weights_imag"]).reshape(FT, 128).T),
        "b2g": np.ascontiguousarray(
            np.asarray(inputs["L0_ffn_b2"]).reshape(PT, 128).T),
        "tri": np.triu(np.ones((128, 128))).astype(ml_dtypes.bfloat16),
        "b2row": np.ascontiguousarray(
            np.asarray(inputs["L0_ffn_b2"]).reshape(1, D)),
        "idm": np.eye(128).astype(ml_dtypes.bfloat16),
        "mcm": np.tril(np.full((128, 128), -1e9), -1).astype(
            ml_dtypes.bfloat16),
    }
    in_maps = []
    for c in range(ncores):
        m = dict(shared)
        xs = x[c * seqs_per_core:(c + 1) * seqs_per_core]
        m["xt"] = np.ascontiguousarray(xs.transpose(0, 2, 1)).astype(
            np.float32, copy=False)
        in_maps.append(m)
    return in_maps


_CACHE = {}


def kernel(**inputs):
    if "nc" not in _CACHE:
        _CACHE["nc"] = build_nc(4)
    nc = _CACHE["nc"]
    in_maps = host_prep(inputs)
    res = run_bass_kernel_spmd(nc, in_maps, core_ids=list(range(NCORES)))
    out = np.empty((32, T, D), dtype=np.float32)
    for c in range(NCORES):
        out[c * 4:(c + 1) * 4] = np.asarray(res.results[c]["yt"]).transpose(0, 2, 1)
    return out
